# version 07
# baseline (speedup 1.0000x reference)
import numpy as np
from contextlib import ExitStack

import concourse.bass as bass
import concourse.mybir as mybir
from concourse.bass_utils import run_bass_kernel_spmd

F32 = mybir.dt.float32
BF16 = mybir.dt.bfloat16
I32 = mybir.dt.int32
AF = mybir.ActivationFunctionType
ALU = mybir.AluOpType
AX = mybir.AxisListType

T = 2048
DM = 1024
NS = 16
RK = 64
II = 2816
DC = DM // 128
NI = II // 128
TCS = 256
NTC = T // TCS
NT128 = T // 128

ENGINES = ("sync", "vector", "scalar", "tensor", "gpsimd")


class Plan:
    def __init__(self, lockstep=False):
        self.items = {e: [] for e in ENGINES}
        self.cnt = {}
        self.lockstep = lockstep

    def new_sem(self, name):
        self.cnt[name] = 0

    def op(self, eng, fn, sem=None, k=1):
        if sem is not None:
            self.cnt[sem] += k
        self.items[eng].append(("op", fn, sem, k))
        if self.lockstep and sem is not None:
            v = self.cnt[sem]
            for e2 in ENGINES:
                if e2 != eng:
                    self.items[e2].append(("wait", sem, v))

    def wait(self, eng, sem, val=None):
        v = self.cnt[sem] if val is None else val
        self.items[eng].append(("wait", sem, v))

    def mark(self, sem):
        return self.cnt[sem]


def build(nc, phases=("W", "M", "F"), m_tc=NTC, m_stage=99, dbg_tc=-1, dbg_s=0, dbg_dc=0, lockstep=False):
    x_d = nc.dram_tensor("x", [T, DM], F32, kind="ExternalInput").ap()
    wxT_d = nc.dram_tensor("wxT", [128, DC * 96], F32, kind="ExternalInput").ap()
    wdtT_d = nc.dram_tensor("wdtT", [RK, DM], F32, kind="ExternalInput").ap()
    bdt_d = nc.dram_tensor("bdt", [128, DC], F32, kind="ExternalInput").ap()
    alog_d = nc.dram_tensor("alog", [128, DC * NS], F32, kind="ExternalInput").ap()
    dvec_d = nc.dram_tensor("dvec", [128, DC], F32, kind="ExternalInput").ap()
    wg_d = nc.dram_tensor("wg", [II, DM], F32, kind="ExternalInput").ap()
    wu_d = nc.dram_tensor("wu", [II, DM], F32, kind="ExternalInput").ap()
    wd_d = nc.dram_tensor("wd", [DM, II], F32, kind="ExternalInput").ap()
    eyeb_d = nc.dram_tensor("eyeb", [128, 128], BF16, kind="ExternalInput").ap()
    eyef_d = nc.dram_tensor("eyef", [128, 128], F32, kind="ExternalInput").ap()
    selb_d = nc.dram_tensor("selb", [NS, NS * 128], BF16, kind="ExternalInput").ap()
    magic_d = nc.dram_tensor("magic", [128, 1], I32, kind="ExternalInput").ap()
    eps_d = nc.dram_tensor("epsc", [128, 1], F32, kind="ExternalInput").ap()
    out_d = nc.dram_tensor("out", [T, DM], F32, kind="ExternalOutput").ap()
    dbg_d = nc.dram_tensor("dbg", [128, 10112], F32, kind="ExternalOutput").ap()
    x1s_d = nc.dram_tensor("x1scr", [T, DM], F32, kind="ExternalOutput").ap()
    wqds_d = nc.dram_tensor("wqds", [II, DM], BF16, kind="ExternalOutput").ap()

    P = Plan(lockstep=lockstep)
    for s in ("ld", "gld", "vv", "ss", "tt", "so", "gc"):
        P.new_sem(s)

    ctx = ExitStack()
    SB = lambda n, sh, dt=F32: ctx.enter_context(nc.sbuf_tensor("s_" + n, sh, dt))

    wqTg = SB("wqTg", [128, DC * II], BF16)
    wqTu = SB("wqTu", [128, DC * II], BF16)
    wxTh = SB("wxTh", [128, DC * 96], BF16)
    wxTl = SB("wxTl", [128, DC * 96], BF16)
    wdtT = SB("wdtT", [RK, DM], BF16)
    bdt = SB("bdt", [128, DC])
    dcol = SB("dcol", [128, DC])
    alog = SB("alog", [128, DC * NS])
    aneg = SB("aneg", [128, DC * NS])
    eyeb = SB("eyeb", [128, 128], BF16)
    eyef = SB("eyef", [128, 128], F32)
    n_mag = SB("n_mag", [128, 1], I32)
    epsc = SB("epsc", [128, 1])
    state = SB("state", [128, DC * NS])
    bnbuf = SB("bnbuf", [128, 12])
    bnagg = SB("bnagg", [128, 2])
    rstd = SB("rstd", [128, 1])
    lntmp = SB("lntmp", [128, 1])
    n_b2 = SB("n_b2", [128, 1], I32)
    n_y = SB("n_y", [128, 1])
    n_t1 = SB("n_t1", [128, 1])
    n_s = SB("n_s", [128, 1])

    AR1 = SB("AR1", [128, 15104])
    AR2 = SB("AR2", [128, 5632])
    AR3 = SB("AR3", [128, 7680])

    def cv(ar, c0, c1, dt=F32, p0=0, p1=128):
        v = ar[p0:p1, c0:c1]
        return v if dt == F32 else v.bitcast(dt)

    QW = dict(kf=cv(AR1, 0, 2816, I32), ku=cv(AR1, 2816, 5632, I32),
              kv=cv(AR1, 5632, 8448, I32), acc=cv(AR1, 8448, 11264),
              tmp=cv(AR1, 11264, 14080), m=cv(AR1, 14080, 14256),
              sc=cv(AR1, 14256, 14432), rs=cv(AR1, 14432, 14608),
              hs=cv(AR1, 14608, 14784), cap=2816)
    QF = dict(kf=cv(AR1, 0, 1024, I32), ku=cv(AR1, 1024, 2048, I32),
              kv=cv(AR1, 2048, 3072, I32), acc=cv(AR1, 3072, 4096),
              tmp=cv(AR1, 4096, 5120), m=cv(AR1, 5120, 5184),
              sc=cv(AR1, 5184, 5248), rs=cv(AR1, 5248, 5312),
              hs=cv(AR1, 5312, 5376), cap=1024)
    wq_sl = [cv(AR1, 5376 + 512 * i, 5888 + 512 * i, BF16) for i in range(19)] + \
            [cv(AR2, 3072 + 512 * i, 3584 + 512 * i, BF16) for i in range(3)]

    xfm32 = cv(AR1, 0, 2048)
    dtfm = cv(AR1, 2048, 4096)
    ufm = cv(AR1, 4096, 6144)
    yfm = cv(AR1, 6144, 8192)
    xfmhi = cv(AR1, 8192, 9216, BF16)
    xfmlo = cv(AR1, 12976, 14000, BF16)
    abar = [cv(AR1, 9216 + 260 * i, 9473 + 260 * i) for i in range(2)]
    tokb = [cv(AR1, 9736 + 260 * i, 9993 + 260 * i) for i in range(2)]
    hbuf = [cv(AR1, 10256 + 260 * i, 10513 + 260 * i) for i in range(2)]
    zbuf = [cv(AR1, 10776 + 260 * i, 11032 + 260 * i) for i in range(2)]
    bmvf = cv(AR1, 11296, 11552, F32, 0, 32)
    cmvf = cv(AR1, 11552, 11808, F32, 0, 32)
    bm_t = [cv(AR1, 11808 + 128 * j, 11936 + 128 * j, BF16, 0, NS) for j in range(3)]
    cm_t = [cv(AR1, 12192, 12320, BF16, 0, NS),
            cv(AR1, 12720, 12848, BF16, 0, NS),
            cv(AR1, 12848, 12976, BF16, 0, NS)]
    xdl = cv(AR1, 12336, 12464, BF16, 0, RK)
    e1v = cv(AR1, 12464, 12720)
    spl = cv(AR1, 12464, 12720, F32, 0, NS)

    wraw = [cv(AR2, 2816 * i, 2816 * (i + 1)) for i in range(2)]
    xtm = [cv(AR2, 1024 * i, 1024 * (i + 1)) for i in range(2)]
    x1t = [cv(AR2, 2048 + 1024 * i, 3072 + 1024 * i) for i in range(2)]
    f_x1 = [cv(AR2, 1024 * i, 1024 * (i + 1)) for i in range(2)]
    f_o = cv(AR2, 2048, 3072)

    q_q = cv(AR3, 0, 1408, BF16)
    selm = cv(AR3, 6240, 7264, BF16, 0, NS)
    abig = [cv(AR3, 0, 2080), cv(AR3, 2080, 4160)]
    tbig = cv(AR3, 4160, 6240)
    hbig = cv(AR1, 9216, 11296)
    f_h1 = cv(AR3, 0, 2816)
    f_r2 = cv(AR3, 2816, 3840)
    f_hq = cv(AR3, 3840, 5248, BF16)
    f_hqT = cv(AR3, 5248, 6656, BF16)
    f_xq = cv(AR3, 6656, 7168, BF16)
    f_xqT = cv(AR3, 7168, 7680, BF16)
    wdev = cv(AR3, 1408, 1920, BF16)

    ps = ctx.enter_context(nc.psum_tensor("ps", [128, 4096], F32))
    psb = lambda b: ps[:, 512 * b: 512 * (b + 1)]
    psbf = lambda b: ps[:, 512 * b: 512 * (b + 1)].bitcast(BF16)

    fls = SB("fls", [128, 4])

    def vfence():
        P.wait("vector", "vv")
        P.op("vector", lambda E: E.tensor_copy(out=fls[:, 0:1], in_=epsc[:]), "vv", 1)

    def sfence():
        P.wait("scalar", "ss")
        P.op("scalar", lambda E: E.copy(out=fls[:, 1:2], in_=epsc[:]), "ss", 1)

    def gfence():
        P.wait("gpsimd", "gc")
        P.op("gpsimd", lambda E: E.tensor_copy(out=fls[:, 2:3], in_=epsc[:]), "gc", 1)

    P.op("sync", lambda E: E.dma_start(out=bdt[:], in_=bdt_d[:]), "ld", 16)
    P.op("sync", lambda E: E.dma_start(out=dcol[:], in_=dvec_d[:]), "ld", 16)
    P.op("sync", lambda E: E.dma_start(out=alog[:], in_=alog_d[:]), "ld", 16)
    P.op("sync", lambda E: E.dma_start(out=eyeb[:], in_=eyeb_d[:]), "ld", 16)
    P.op("sync", lambda E: E.dma_start(out=eyef[:], in_=eyef_d[:]), "ld", 16)
    P.op("sync", lambda E: E.dma_start(out=selm, in_=selb_d[:]), "ld", 16)
    P.op("sync", lambda E: E.dma_start(out=n_mag[:], in_=magic_d[:]), "ld", 16)
    P.op("sync", lambda E: E.dma_start(out=epsc[:], in_=eps_d[:]), "ld", 16)
    wxTtmp = cv(AR3, 0, DC * 96)
    P.op("sync", lambda E: E.dma_start(out=wxTtmp, in_=wxT_d[:]), "ld", 16)
    P.op("gpsimd", lambda E: E.dma_start(out=wdtT[:], in_=wdtT_d[:]), "gld", 16)
    P.wait("scalar", "ld")
    P.op("scalar", lambda E: E.activation(out=aneg[:], in_=alog[:], func=AF.Exp), "ss", 1)
    P.wait("vector", "ss")
    P.op("vector", lambda E: E.tensor_scalar(out=aneg[:], in0=aneg[:], scalar1=-1.0,
                                             scalar2=None, op0=ALU.mult), "vv", 1)
    P.op("vector", lambda E: E.memset(state[:], 0.0), "vv", 1)
    P.wait("vector", "ld")
    P.op("vector", lambda E: E.tensor_copy(out=wxTh[:], in_=wxTtmp), "vv", 1)
    P.op("vector", lambda E: E.tensor_tensor(out=wxTl[:], in0=wxTtmp, in1=wxTh[:],
                                             op=ALU.subtract), "vv", 1)
    vfence()
    P.wait("scalar", "vv")

    def quant(src_ap, dst_ap, nfree, qs=None):
        qs = qs or QW
        for c0 in range(0, nfree, qs["cap"]):
            cw = min(qs["cap"], nfree - c0)
            nblk = cw // 16
            mm_, sc_ = qs["m"][:, 0:nblk], qs["sc"][:, 0:nblk]
            rs_, hs_ = qs["rs"][:, 0:nblk], qs["hs"][:, 0:nblk]
            xn_ = qs["acc"][:, 0:cw]
            kf_, ku_, kv_ = qs["kf"][:, 0:cw], qs["ku"][:, 0:cw], qs["kv"][:, 0:cw]
            acc_, tmp_ = qs["acc"][:, 0:cw], qs["tmp"][:, 0:cw]
            src3 = src_ap[:, c0:c0 + cw].rearrange("p (b k) -> p b k", k=16)
            dst3 = dst_ap[:, c0:c0 + cw].rearrange("p (b k) -> p b k", k=16)
            rs_b = rs_.unsqueeze(2).broadcast_to((128, nblk, 16))
            hs_b = hs_.unsqueeze(2).broadcast_to((128, nblk, 16))

            P.op("vector", lambda E, mm_=mm_, src3=src3: E.tensor_reduce(
                out=mm_, in_=src3, axis=AX.X, op=ALU.max,
                apply_absolute_value=True), "vv", 1)
            P.wait("vector", "vv")
            P.op("vector", lambda E, sc_=sc_, mm_=mm_: E.tensor_scalar(
                out=sc_, in0=mm_, scalar1=1.0 / 6.0,
                scalar2=1e-12, op0=ALU.mult, op1=ALU.max), "vv", 1)
            P.wait("vector", "vv")
            P.op("vector", lambda E, rs_=rs_, sc_=sc_: E.reciprocal(out=rs_, in_=sc_), "vv", 1)
            P.op("vector", lambda E, hs_=hs_, sc_=sc_: E.tensor_scalar(
                out=hs_, in0=sc_, scalar1=0.5, scalar2=None, op0=ALU.mult), "vv", 1)
            P.wait("vector", "vv")
            P.op("vector", lambda E, xn_=xn_, src3=src3, rs_b=rs_b: E.tensor_tensor(
                out=xn_.rearrange("p (b k) -> p b k", k=16),
                in0=src3, in1=rs_b, op=ALU.mult), "vv", 1)
            P.wait("scalar", "vv")
            P.op("scalar", lambda E, kf_=kf_, xn_=xn_: E.activation(
                out=kf_, in_=xn_, func=AF.Identity, scale=2.0), "ss", 1)
            P.op("scalar", lambda E, ku_=ku_, xn_=xn_: E.activation(
                out=ku_, in_=xn_, func=AF.Identity, scale=1.0), "ss", 1)
            P.op("scalar", lambda E, kv_=kv_, xn_=xn_: E.activation(
                out=kv_, in_=xn_, func=AF.Identity, scale=0.5), "ss", 1)
            P.wait("vector", "ss")
            P.op("vector", lambda E, acc_=acc_, kf_=kf_: E.tensor_scalar(
                out=acc_, in0=kf_, scalar1=4.0, scalar2=-4.0,
                op0=ALU.min, op1=ALU.max), "vv", 1)
            for sk, lo, mul in ((ku_, 4.0, 2.0), (ku_, 2.0, -2.0),
                                (kv_, 3.0, 4.0), (kv_, 2.0, -4.0)):
                P.op("vector", lambda E, sk=sk, lo=lo, tmp_=tmp_: E.tensor_scalar(
                    out=tmp_, in0=sk, scalar1=lo, scalar2=-lo, op0=ALU.min, op1=ALU.max), "vv", 1)
                P.op("vector", lambda E, mul=mul, acc_=acc_, tmp_=tmp_: E.scalar_tensor_tensor(
                    out=acc_, in0=tmp_, scalar=mul, in1=acc_, op0=ALU.mult, op1=ALU.add), "vv", 1)
            P.op("vector", lambda E, dst3=dst3, acc_=acc_, hs_b=hs_b: E.tensor_tensor(
                out=dst3, in0=acc_.rearrange("p (b k) -> p b k", k=16),
                in1=hs_b, op=ALU.mult), "vv", 1)
        vfence()

    jobs = []
    if "W" in phases:
        for wi, (wsrc, is_down) in enumerate(((wg_d, False), (wu_d, False), (wd_d, True))):
            for ni in range(DC if is_down else NI):
                jobs.append((wi, wsrc, is_down, ni))
    quant_done = []
    for j, (wi, wsrc, is_down, ni) in enumerate(jobs):
        buf = wraw[j % 2]
        nfree = II if is_down else DM
        if j >= 2:
            P.wait("sync", "vv", quant_done[j - 2])
        P.op("sync", lambda E, buf=buf, wsrc=wsrc, ni=ni, nf=nfree:
             E.dma_start(out=buf[:, 0:nf], in_=wsrc[ni * 128:(ni + 1) * 128, :]), "ld", 16)
        P.wait("vector", "ld")
        P.wait("vector", "tt")
        P.wait("vector", "gld")
        quant(buf[:, 0:nfree], q_q[:, 0:nfree], nfree)
        quant_done.append(P.mark("vv"))
        ntp = nfree // 128
        P.wait("tensor", "vv", quant_done[-1])
        for g0 in range(0, ntp, 8):
            gn = min(8, ntp - g0)
            if g0 > 0:
                P.wait("tensor", "vv")
            for k in range(gn):
                kk = g0 + k
                P.op("tensor", lambda E, kk=kk, k=k, f=(k == 0), l=(k == gn - 1):
                     E.matmul(out=psbf(6)[:, 128 * k:128 * (k + 1)],
                              lhsT=q_q[:, 128 * kk:128 * (kk + 1)], rhs=eyeb[:],
                              is_transpose=True, start=f, stop=l),
                     "tt", 1 if k == gn - 1 else 0)
            P.wait("vector", "tt")
            if not is_down:
                dst = wqTg if wi == 0 else wqTu
                def ev(E, dst=dst, g0=g0, gn=gn, ni=ni):
                    o = dst[:].rearrange("p (c n) -> p c n", n=II)[:, g0:g0 + gn,
                                                                   ni * 128:(ni + 1) * 128]
                    return E.tensor_copy(out=o, in_=psbf(6)[:, 0:128 * gn]
                                         .rearrange("p (c n) -> p c n", n=128))
                P.op("vector", ev, "vv", 1)
            else:
                P.wait("vector", "gld")
                P.op("vector", lambda E, gn=gn: E.tensor_copy(
                    out=wdev[:, 0:128 * gn], in_=psbf(6)[:, 0:128 * gn]), "vv", 1)
                vfence()
                P.wait("gpsimd", "vv")
                def st(E, g0=g0, gn=gn, ni=ni):
                    o = wqds_d.rearrange("(g j) d -> j g d", j=128)[:, g0:g0 + gn,
                                                                   ni * 128:(ni + 1) * 128]
                    return E.dma_start(out=o, in_=wdev[:, 0:128 * gn]
                                       .rearrange("p (g n) -> p g n", n=128))
                P.op("gpsimd", st, "gld", 16)

    store_mark = [0]
    for tc in range(min(NTC, m_tc) if "M" in phases else 0):
        t0 = tc * TCS
        P.wait("sync", "vv")
        P.wait("sync", "gld")
        P.wait("vector", "tt")
        P.wait("vector", "so", store_mark[-1])
        P.wait("tensor", "vv")
        P.wait("tensor", "ss")
        for i in range(2):
            P.op("sync", lambda E, i=i, t0=t0:
                 E.dma_start(out=xtm[i], in_=x_d[t0 + 128 * i: t0 + 128 * (i + 1), :]),
                 "ld", 16)
        P.wait("tensor", "ld")
        if m_stage < -4:
            continue
        for dc in range(DC):
            b = dc % 2
            if dc >= 2:
                P.wait("tensor", "vv")
                P.wait("tensor", "ss")
            if m_stage < -3 and dc >= 1:
                continue
            for i in range(2):
                P.op("tensor", lambda E, dc=dc, i=i, b=b:
                     E.matmul(out=psb(b)[:, 128 * i:128 * (i + 1)],
                              lhsT=xtm[i][:, 128 * dc:128 * (dc + 1)], rhs=eyef[:],
                              is_transpose=True, start=(i == 0), stop=(i == 1)),
                     "tt", 1 if i == 1 else 0)
            P.wait("scalar", "tt")
            P.op("scalar", lambda E, dc=dc, b=b:
                 E.copy(out=xfm32[:, TCS * dc: TCS * (dc + 1)], in_=psb(b)[:, 0:TCS]), "ss", 1)
            sfence()
        if m_stage < -2:
            continue
        P.wait("vector", "ss")
        P.op("vector", lambda E: E.tensor_copy(out=xfmhi[:], in_=xfm32[:]), "vv", 1)
        P.op("vector", lambda E: E.tensor_tensor(out=xfmlo[:], in0=xfm32[:], in1=xfmhi[:],
                                                 op=ALU.subtract), "vv", 1)
        vfence()
        P.wait("tensor", "ss")
        P.wait("tensor", "vv")
        for j, (wop, xop) in enumerate(((wxTh, xfmhi), (wxTh, xfmlo), (wxTl, xfmhi))):
            for dc in range(DC):
                P.op("tensor", lambda E, dc=dc, wop=wop, xop=xop, j=j:
                     E.matmul(out=psb(2)[0:80, 0:TCS], lhsT=wop[:, 96 * dc:96 * dc + 80],
                              rhs=xop[:, TCS * dc:TCS * (dc + 1)],
                              start=(dc == 0 and j == 0), stop=(dc == DC - 1 and j == 2)),
                     "tt", 0)
                P.op("tensor", lambda E, dc=dc, wop=wop, xop=xop, j=j:
                     E.matmul(out=psb(3)[0:16, 0:TCS], lhsT=wop[:, 96 * dc + 80:96 * (dc + 1)],
                              rhs=xop[:, TCS * dc:TCS * (dc + 1)],
                              start=(dc == 0 and j == 0), stop=(dc == DC - 1 and j == 2)),
                     "tt", 1 if (dc == DC - 1 and j == 2) else 0)
        P.wait("vector", "tt")
        P.op("vector", lambda E: E.tensor_copy(out=xdl[:, 0:TCS], in_=psb(2)[0:RK, 0:TCS]),
             "vv", 1)
        P.op("vector", lambda E: E.tensor_copy(out=bmvf[:, 0:TCS], in_=psb(2)[RK:96, 0:TCS]),
             "vv", 1)
        P.op("vector", lambda E: E.tensor_copy(out=cmvf[:, 0:TCS], in_=psb(3)[0:32, 0:TCS]),
             "vv", 1)
        P.wait("vector", "vv")
        for vf, terms in ((bmvf, bm_t), (cmvf, cm_t)):
            P.op("vector", lambda E, vf=vf, t0_=terms[0]:
                 E.tensor_copy(out=t0_[:, 0:TCS], in_=vf[0:NS, 0:TCS]), "vv", 1)
            P.wait("vector", "vv")
            P.op("vector", lambda E, vf=vf, t0_=terms[0]:
                 E.tensor_tensor(out=spl[:, 0:TCS], in0=vf[0:NS, 0:TCS], in1=t0_[:, 0:TCS],
                                 op=ALU.subtract), "vv", 1)
            P.wait("vector", "vv")
            P.op("vector", lambda E, t1_=terms[1]:
                 E.tensor_copy(out=t1_[:, 0:TCS], in_=spl[:, 0:TCS]), "vv", 1)
            P.wait("vector", "vv")
            P.op("vector", lambda E, t1_=terms[1], t2_=terms[2]:
                 E.tensor_tensor(out=t2_[:, 0:TCS], in0=spl[:, 0:TCS], in1=t1_[:, 0:TCS],
                                 op=ALU.subtract), "vv", 1)
        vfence()
        if m_stage < -1:
            continue
        P.wait("tensor", "ss")
        P.wait("tensor", "vv")
        for dc in range(DC):
            b = 3 if dc % 2 == 0 else 2
            if dc >= 2:
                P.wait("tensor", "ss")
            P.op("tensor", lambda E, dc=dc, b=b:
                 E.matmul(out=psb(b)[:, 0:TCS], lhsT=wdtT[:, 128 * dc:128 * (dc + 1)],
                          rhs=xdl[:, 0:TCS], start=True, stop=True), "tt", 1)
            P.wait("scalar", "tt")
            P.op("scalar", lambda E, dc=dc, b=b:
                 E.activation(out=e1v[:], in_=psb(b)[:, 0:TCS], func=AF.Exp,
                              bias=bdt[:, dc:dc + 1], scale=1.0), "ss", 1)
            P.op("scalar", lambda E, dc=dc:
                 E.activation(out=dtfm[:, TCS * dc:TCS * (dc + 1)], in_=e1v[:], func=AF.Ln,
                              bias=1.0, scale=1.0), "ss", 1)
            sfence()
            P.wait("vector", "ss")
            P.op("vector", lambda E, dc=dc:
                 E.tensor_tensor(out=ufm[:, TCS * dc:TCS * (dc + 1)],
                                 in0=dtfm[:, TCS * dc:TCS * (dc + 1)],
                                 in1=xfm32[:, TCS * dc:TCS * (dc + 1)], op=ALU.mult), "vv", 1)
        if m_stage < 1:
            continue
        if tc == 0:
            for i in range(2):
                P.op("vector", lambda E, i=i: E.memset(
                    abig[i].rearrange("p (c n) -> p c n", n=260)[:, :, 0:1], 0.0), "vv", 1)
        tb3 = tbig.rearrange("p (c n) -> p c n", n=260)
        hb3 = hbig.rearrange("p (c n) -> p c n", n=260)
        uf3 = ufm.rearrange("p (c n) -> p c n", n=TCS)
        yf3 = yfm.rearrange("p (c n) -> p c n", n=TCS)
        st3 = state.rearrange("p (c n) -> p c n", n=NS)
        scan_marks = {}
        zread_marks = {}
        bc_marks = {}
        gyac = {}
        for s in range(NS):
            pb = s % 2
            ab = abig[pb]
            ab3 = ab.rearrange("p (c n) -> p c n", n=260)
            if s >= 2:
                P.wait("tensor", "vv", zread_marks[s - 2])
            for bank, terms in ((4 + pb, bm_t), (6 + pb, cm_t)):
                for j in range(3):
                    P.op("tensor", lambda E, s=s, bank=bank, tj=terms[j],
                         f=(j == 0), l=(j == 2):
                         E.matmul(out=psb(bank)[:, 0:TCS], lhsT=selm[:, 128 * s:128 * (s + 1)],
                                  rhs=tj[:, 0:TCS], start=f, stop=l),
                         "tt", 1 if j == 2 else 0)
            bc_marks[s] = P.mark("tt")
            if s >= 2:
                P.wait("scalar", "vv", scan_marks[s - 2])
            for dc in range(DC):
                P.op("scalar", lambda E, dc=dc, s=s, ab3=ab3:
                     E.activation(out=ab3[:, dc, 1:TCS + 1],
                                  in_=dtfm[:, TCS * dc:TCS * (dc + 1)],
                                  func=AF.Exp, scale=aneg[:, dc * NS + s:dc * NS + s + 1]),
                     "ss", 1)
            sfence()
            if s >= 1:
                P.wait("vector", "gc", gyac[s - 1])
            P.op("vector", lambda E, s=s:
                 E.tensor_copy(out=tb3[:, :, 0:1], in_=st3[:, :, s:s + 1]), "vv", 1)
            P.wait("vector", "tt", bc_marks[s])
            P.op("vector", lambda E, pb=pb:
                 E.tensor_tensor(out=tb3[:, :, 1:TCS + 1], in0=uf3[:, :, :],
                                 in1=psb(4 + pb)[:, 0:TCS].unsqueeze(1)
                                 .broadcast_to((128, DC, TCS)),
                                 op=ALU.mult), "vv", 1)
            P.wait("vector", "ss")
            for dc in range(DC):
                P.op("vector", lambda E, dc=dc, ab=ab:
                     E.tensor_tensor_scan(out=hbig[:, 260 * dc:260 * dc + 257],
                                          data0=ab[:, 260 * dc:260 * dc + 257],
                                          data1=tbig[:, 260 * dc:260 * dc + 257],
                                          initial=0.0, op0=ALU.mult, op1=ALU.add), "vv", 1)
            scan_marks[s] = P.mark("vv")
            P.op("vector", lambda E, pb=pb:
                 E.tensor_tensor(out=tb3[:, :, 1:TCS + 1], in0=hb3[:, :, 1:TCS + 1],
                                 in1=psb(6 + pb)[:, 0:TCS].unsqueeze(1)
                                 .broadcast_to((128, DC, TCS)),
                                 op=ALU.mult), "vv", 1)
            zread_marks[s] = P.mark("vv")
            P.op("vector", lambda E, s=s:
                 E.tensor_copy(out=st3[:, :, s:s + 1], in_=hb3[:, :, TCS:TCS + 1]), "vv", 1)
            P.wait("gpsimd", "vv", zread_marks[s])
            if s == 0:
                P.op("gpsimd", lambda E:
                     E.tensor_copy(out=yf3[:, :, :], in_=tb3[:, :, 1:TCS + 1]), "gc", 1)
            else:
                P.op("gpsimd", lambda E:
                     E.tensor_tensor(out=yf3[:, :, :], in0=yf3[:, :, :],
                                     in1=tb3[:, :, 1:TCS + 1], op=ALU.add), "gc", 1)
            gyac[s] = P.mark("gc")
        if m_stage < 2:
            continue
        gfence()
        P.wait("vector", "gc")
        for dc in range(DC):
            P.op("vector", lambda E, dc=dc:
                 E.scalar_tensor_tensor(out=yfm[:, TCS * dc:TCS * (dc + 1)],
                                        in0=xfm32[:, TCS * dc:TCS * (dc + 1)],
                                        scalar=dcol[:, dc:dc + 1],
                                        in1=yfm[:, TCS * dc:TCS * (dc + 1)],
                                        op0=ALU.mult, op1=ALU.add), "vv", 1)
        vfence()
        P.wait("tensor", "vv")
        for dc in range(DC):
            b = dc % 2
            if dc >= 2:
                P.wait("tensor", "vv")
            for i in range(2):
                P.op("tensor", lambda E, dc=dc, i=i, b=b:
                     E.matmul(out=psb(b)[:, 128 * i:128 * (i + 1)],
                              lhsT=yfm[:, TCS * dc + 128 * i: TCS * dc + 128 * (i + 1)],
                              rhs=eyef[:], is_transpose=True, start=(i == 0), stop=(i == 1)),
                     "tt", 1 if i == 1 else 0)
            P.wait("vector", "tt")
            for i in range(2):
                P.op("vector", lambda E, dc=dc, i=i, b=b:
                     E.tensor_tensor(out=xtm[i][:, 128 * dc:128 * (dc + 1)],
                                     in0=psb(b)[:, 128 * i:128 * (i + 1)],
                                     in1=xtm[i][:, 128 * dc:128 * (dc + 1)], op=ALU.add),
                     "vv", 1)
        if tc == dbg_tc:
            P.wait("gpsimd", "vv")
            P.op("gpsimd", lambda E: E.dma_start(out=dbg_d[:, 8576:9600], in_=xtm[0]), "so", 16)
            P.op("gpsimd", lambda E: E.dma_start(out=dbg_d[:, 9600:10112],
                                                 in_=xtm[1][:, 0:512]), "so", 16)
            P.wait("vector", "so")
        for i in range(2):
            P.op("vector", lambda E, i=i: E.bn_stats(out=bnbuf[:, 0:6], in_=xtm[i][:, 0:512]),
                 "vv", 1)
            P.op("vector", lambda E, i=i: E.bn_stats(out=bnbuf[:, 6:12], in_=xtm[i][:, 512:1024]),
                 "vv", 1)
            P.wait("vector", "vv")
            P.op("vector", lambda E: E.bn_aggr(out=bnagg[:], in_=bnbuf[:]), "vv", 1)
            vfence()
            P.wait("scalar", "vv")
            P.op("scalar", lambda E: E.activation(out=lntmp[:], in_=bnagg[:, 1:2], func=AF.Ln,
                                                  bias=epsc[:], scale=1.0), "ss", 1)
            P.wait("scalar", "ss")
            P.op("scalar", lambda E: E.activation(out=rstd[:], in_=lntmp[:], func=AF.Exp,
                                                  scale=-0.5), "ss", 1)
            sfence()
            if tc == dbg_tc and i == 0:
                P.wait("gpsimd", "ss")
                P.wait("gpsimd", "vv")
                P.op("gpsimd", lambda E: E.dma_start(out=dbg_d[:, 8320:8322], in_=bnagg[:]),
                     "so", 16)
                P.op("gpsimd", lambda E: E.dma_start(out=dbg_d[:, 8322:8323], in_=rstd[:], allow_slow_non_contiguous=True),
                     "so", 16)
                P.wait("vector", "so")
                P.wait("scalar", "so")
            P.wait("vector", "ss")
            P.wait("vector", "vv")
            P.op("vector", lambda E, i=i:
                 E.tensor_scalar(out=x1t[i], in0=xtm[i], scalar1=bnagg[:, 0:1],
                                 scalar2=rstd[:], op0=ALU.subtract, op1=ALU.mult), "vv", 1)
            vfence()
            P.wait("gpsimd", "vv")
            P.op("gpsimd", lambda E, i=i, t0=t0:
                 E.dma_start(out=x1s_d[t0 + 128 * i: t0 + 128 * (i + 1), :], in_=x1t[i]),
                 "so", 16)
        store_mark.append(P.mark("so"))

    nbw = [512, 512, 512, 512, 512, 256]
    f_resid_mark = []
    f_store_mark = []

    def f_head(t2):
        fb = f_x1[t2 % 2]
        if t2 >= 2:
            P.wait("sync", "vv", f_resid_mark[t2 - 2])
        if t2 == 0:
            P.wait("sync", "so")
        P.op("sync", lambda E, t2=t2, fb=fb:
             E.dma_start(out=fb, in_=x1s_d[128 * t2:128 * (t2 + 1), :]), "ld", 16)
        P.wait("vector", "ld")
        if t2 == 0:
            P.wait("vector", "tt")
            P.wait("vector", "so", store_mark[-1])
        quant(fb, f_xq, DM, QF)
        P.wait("tensor", "vv")
        for k in range(DC):
            P.op("tensor", lambda E, k=k:
                 E.matmul(out=psbf(5)[:, 128 * k:128 * (k + 1)],
                          lhsT=f_xq[:, 128 * k:128 * (k + 1)], rhs=eyeb[:],
                          is_transpose=True, start=(k == 0), stop=(k == DC - 1)),
                 "tt", 1 if k == DC - 1 else 0)
        P.wait("vector", "tt")
        P.op("vector", lambda E: E.tensor_copy(out=f_xqT[:], in_=psbf(5)[:, 0:DM]), "vv", 1)
        vfence()
        f_head_mark[t2] = P.mark("vv")

    f_head_mark = {}
    wdld_mark = [0]
    if "F" in phases:
        f_head(0)
        P.wait("sync", "gld")
        for ic in range(NI):
            P.op("sync", lambda E, ic=ic: E.dma_start(
                out=wq_sl[ic], in_=wqds_d[128 * ic:128 * (ic + 1), :]), "ld", 16)
        wdld_mark[0] = P.mark("ld")
    for t2 in range(NT128 if "F" in phases else 0):
        fb = f_x1[t2 % 2]
        P.wait("tensor", "vv", f_head_mark[t2])
        for dc in range(DC):
            for nb in range(6):
                P.op("tensor", lambda E, dc=dc, nb=nb, w=nbw[nb]:
                     E.matmul(out=psb(nb)[:, 0:w], lhsT=f_xqT[:, 128 * dc:128 * (dc + 1)],
                              rhs=wqTg[:, II * dc + 512 * nb: II * dc + 512 * nb + w],
                              start=(dc == 0), stop=(dc == DC - 1)),
                     "tt", 1 if (dc == DC - 1 and nb == 5) else 0)
        P.wait("scalar", "tt")
        for c0 in range(0, II, 1024):
            cw = min(1024, II - c0)
            P.op("scalar", lambda E, c0=c0, cw=cw: E.activation(
                out=QF["acc"][:, 0:cw], in_=ps[:, c0:c0 + cw],
                func=AF.Exp, scale=-1.0), "ss", 1)
            P.op("scalar", lambda E, c0=c0, cw=cw: E.activation(
                out=QF["tmp"][:, 0:cw], in_=QF["acc"][:, 0:cw],
                func=AF.Ln, bias=1.0, scale=1.0), "ss", 1)
            P.op("scalar", lambda E, c0=c0, cw=cw: E.activation(
                out=f_h1[:, c0:c0 + cw], in_=QF["tmp"][:, 0:cw],
                func=AF.Exp, scale=-1.0), "ss", 1)
        P.wait("vector", "ss")
        P.op("vector", lambda E: E.scalar_tensor_tensor(out=f_h1[:], in0=f_h1[:], scalar=1.0,
                                                        in1=ps[:, 0:II], op0=ALU.mult,
                                                        op1=ALU.mult), "vv", 1)
        P.wait("tensor", "vv")
        for dc in range(DC):
            for nb in range(6):
                P.op("tensor", lambda E, dc=dc, nb=nb, w=nbw[nb]:
                     E.matmul(out=psb(nb)[:, 0:w], lhsT=f_xqT[:, 128 * dc:128 * (dc + 1)],
                              rhs=wqTu[:, II * dc + 512 * nb: II * dc + 512 * nb + w],
                              start=(dc == 0), stop=(dc == DC - 1)),
                     "tt", 1 if (dc == DC - 1 and nb == 5) else 0)
        P.wait("vector", "tt")
        P.op("vector", lambda E: E.tensor_tensor(out=f_h1[:], in0=f_h1[:], in1=ps[:, 0:II],
                                                 op=ALU.mult), "vv", 1)
        quant(f_h1[:], f_hq, II, QF)
        P.wait("tensor", "vv")
        for g0 in range(0, NI, 8):
            gn = min(8, NI - g0)
            bb = g0 // 8
            for k in range(gn):
                kk = g0 + k
                P.op("tensor", lambda E, kk=kk, k=k, bb=bb, f=(k == 0), l=(k == gn - 1):
                     E.matmul(out=psbf(bb)[:, 128 * k:128 * (k + 1)],
                              lhsT=f_hq[:, 128 * kk:128 * (kk + 1)], rhs=eyeb[:],
                              is_transpose=True, start=f, stop=l),
                     "tt", 1 if k == gn - 1 else 0)
        P.wait("vector", "tt")
        for g0 in range(0, NI, 8):
            gn = min(8, NI - g0)
            bb = g0 // 8
            P.op("vector", lambda E, g0=g0, gn=gn, bb=bb:
                 E.tensor_copy(out=f_hqT[:, 128 * g0:128 * (g0 + gn)],
                               in_=psbf(bb)[:, 0:128 * gn]), "vv", 1)
        vfence()
        P.wait("tensor", "vv")
        if t2 == 0:
            P.wait("tensor", "ld", wdld_mark[0])
        for ic in range(NI):
            for half in range(2):
                P.op("tensor", lambda E, ic=ic, half=half:
                     E.matmul(out=psb(6 + half)[:, 0:512],
                              lhsT=f_hqT[:, 128 * ic:128 * (ic + 1)],
                              rhs=wq_sl[ic][:, 512 * half:512 * (half + 1)],
                              start=(ic == 0), stop=(ic == NI - 1)),
                     "tt", 1 if half == 1 else 0)
        dn_mark = P.mark("tt")
        if t2 + 1 < NT128:
            f_head(t2 + 1)
        P.wait("vector", "tt", dn_mark)
        P.op("vector", lambda E, fb=fb: E.tensor_tensor(out=f_r2[:], in0=ps[:, 3072:4096],
                                                        in1=fb, op=ALU.add), "vv", 1)
        f_resid_mark.append(P.mark("vv"))
        P.op("vector", lambda E: E.bn_stats(out=bnbuf[:, 0:6], in_=f_r2[:, 0:512]), "vv", 1)
        P.op("vector", lambda E: E.bn_stats(out=bnbuf[:, 6:12], in_=f_r2[:, 512:1024]), "vv", 1)
        P.wait("vector", "vv")
        P.op("vector", lambda E: E.bn_aggr(out=bnagg[:], in_=bnbuf[:]), "vv", 1)
        vfence()
        P.wait("scalar", "vv")
        P.op("scalar", lambda E: E.activation(out=lntmp[:], in_=bnagg[:, 1:2], func=AF.Ln,
                                              bias=epsc[:], scale=1.0), "ss", 1)
        P.wait("scalar", "ss")
        P.op("scalar", lambda E: E.activation(out=rstd[:], in_=lntmp[:], func=AF.Exp,
                                              scale=-0.5), "ss", 1)
        sfence()
        P.wait("vector", "ss")
        if t2 >= 1:
            P.wait("vector", "so", f_store_mark[t2 - 1])
        P.op("vector", lambda E: E.tensor_scalar(out=f_o, in0=f_r2[:],
                                                 scalar1=bnagg[:, 0:1], scalar2=rstd[:],
                                                 op0=ALU.subtract, op1=ALU.mult), "vv", 1)
        vfence()
        P.wait("gpsimd", "vv")
        P.op("gpsimd", lambda E, t2=t2:
             E.dma_start(out=out_d[128 * t2:128 * (t2 + 1), :], in_=f_o), "so", 16)
        f_store_mark.append(P.mark("so"))

    P.wait("gpsimd", "so")

    with ExitStack() as ectx:
        sems = {s: ectx.enter_context(nc.semaphore(s)) for s in P.cnt}
        block = ectx.enter_context(nc.Block())

        def runner(name):
            def go(E):
                for it in P.items[name]:
                    if it[0] == "op":
                        _, fn, sem, k = it
                        inst = fn(E)
                        if sem is not None and k > 0:
                            inst.then_inc(sems[sem], k)
                    else:
                        _, sem, v = it
                        E.wait_ge(sems[sem], v)
            return go

        block.sync(runner("sync"))
        block.vector(runner("vector"))
        block.scalar(runner("scalar"))
        block.tensor(runner("tensor"))
        block.gpsimd(runner("gpsimd"))
    ctx.close()
    return nc


def kernel(L0_attn_x, L0_attn_A_log, L0_attn_D, L0_attn_W_xproj, L0_attn_W_dt,
           L0_attn_b_dt, L0_ffn_w_gate, L0_ffn_w_up, L0_ffn_w_down):
    import ml_dtypes
    x = np.ascontiguousarray(np.asarray(L0_attn_x, dtype=np.float32))
    A_log = np.asarray(L0_attn_A_log, np.float32)
    Dv = np.asarray(L0_attn_D, np.float32)
    Wx = np.asarray(L0_attn_W_xproj, np.float32)
    Wdt = np.asarray(L0_attn_W_dt, np.float32)
    bdt = np.asarray(L0_attn_b_dt, np.float32)
    wg = np.ascontiguousarray(np.asarray(L0_ffn_w_gate, np.float32))
    wu = np.ascontiguousarray(np.asarray(L0_ffn_w_up, np.float32))
    wd = np.ascontiguousarray(np.asarray(L0_ffn_w_down, np.float32))

    wxT = np.ascontiguousarray(
        Wx.T.reshape(DC, 128, 96).transpose(1, 0, 2).reshape(128, DC * 96))
    wdtT = np.ascontiguousarray(Wdt.T)
    bdt_l = np.ascontiguousarray(bdt.reshape(DC, 128).T)
    alog_l = np.ascontiguousarray(
        A_log.reshape(DC, 128, NS).transpose(1, 0, 2).reshape(128, DC * NS))
    dvec_l = np.ascontiguousarray(Dv.reshape(DC, 128).T)
    eyeb = np.eye(128, dtype=ml_dtypes.bfloat16)
    eyef = np.eye(128, dtype=np.float32)
    selb = np.zeros((NS, NS * 128), dtype=ml_dtypes.bfloat16)
    for s in range(NS):
        selb[s, s * 128:(s + 1) * 128] = 1.0
    magic = np.full((128, 1), 0x5F3759DF, dtype=np.int32)

    nc = bass.Bass("TRN2", target_bir_lowering=False, detect_race_conditions=False)
    build(nc)

    in_maps = []
    for b in range(8):
        in_maps.append({
            "x": np.ascontiguousarray(x[b]),
            "wxT": wxT, "wdtT": wdtT, "bdt": bdt_l, "alog": alog_l, "dvec": dvec_l,
            "wg": wg, "wu": wu, "wd": wd,
            "eyeb": eyeb, "eyef": eyef, "selb": selb, "magic": magic,
            "epsc": np.full((128, 1), 1e-5, np.float32),
        })
    res = run_bass_kernel_spmd(nc, in_maps, core_ids=list(range(8)))
    return np.stack([r["out"] for r in res.results], axis=0)


# version 08
# speedup vs baseline: 1.2717x; 1.2717x over previous
import numpy as np
from contextlib import ExitStack

import concourse.bass as bass
import concourse.mybir as mybir
from concourse.bass_utils import run_bass_kernel_spmd

F32 = mybir.dt.float32
BF16 = mybir.dt.bfloat16
I32 = mybir.dt.int32
AF = mybir.ActivationFunctionType
ALU = mybir.AluOpType
AX = mybir.AxisListType

T = 2048
DM = 1024
NS = 16
RK = 64
II = 2816
DC = DM // 128
NI = II // 128
TCS = 256
NTC = T // TCS
NT128 = T // 128

ENGINES = ("sync", "vector", "scalar", "tensor", "gpsimd")


class Plan:
    def __init__(self, lockstep=False):
        self.items = {e: [] for e in ENGINES}
        self.cnt = {}
        self.lockstep = lockstep

    def new_sem(self, name):
        self.cnt[name] = 0

    def op(self, eng, fn, sem=None, k=1):
        if sem is not None:
            self.cnt[sem] += k
        self.items[eng].append(("op", fn, sem, k))
        if self.lockstep and sem is not None:
            v = self.cnt[sem]
            for e2 in ENGINES:
                if e2 != eng:
                    self.items[e2].append(("wait", sem, v))

    def wait(self, eng, sem, val=None):
        v = self.cnt[sem] if val is None else val
        self.items[eng].append(("wait", sem, v))

    def mark(self, sem):
        return self.cnt[sem]


def build(nc, phases=("W", "M", "F"), m_tc=NTC, m_stage=99, dbg_tc=-1, dbg_s=0, dbg_dc=0, lockstep=False):
    x_d = nc.dram_tensor("x", [T, DM], F32, kind="ExternalInput").ap()
    wxT_d = nc.dram_tensor("wxT", [128, DC * 96], F32, kind="ExternalInput").ap()
    wdtT_d = nc.dram_tensor("wdtT", [RK, DM], F32, kind="ExternalInput").ap()
    bdt_d = nc.dram_tensor("bdt", [128, DC], F32, kind="ExternalInput").ap()
    alog_d = nc.dram_tensor("alog", [128, DC * NS], F32, kind="ExternalInput").ap()
    dvec_d = nc.dram_tensor("dvec", [128, DC], F32, kind="ExternalInput").ap()
    wg_d = nc.dram_tensor("wg", [II, DM], F32, kind="ExternalInput").ap()
    wu_d = nc.dram_tensor("wu", [II, DM], F32, kind="ExternalInput").ap()
    wd_d = nc.dram_tensor("wd", [DM, II], F32, kind="ExternalInput").ap()
    eyeb_d = nc.dram_tensor("eyeb", [128, 128], BF16, kind="ExternalInput").ap()
    eyef_d = nc.dram_tensor("eyef", [128, 128], F32, kind="ExternalInput").ap()
    selb_d = nc.dram_tensor("selb", [NS, NS * 128], BF16, kind="ExternalInput").ap()
    magic_d = nc.dram_tensor("magic", [128, 1], I32, kind="ExternalInput").ap()
    eps_d = nc.dram_tensor("epsc", [128, 1], F32, kind="ExternalInput").ap()
    out_d = nc.dram_tensor("out", [T, DM], F32, kind="ExternalOutput").ap()
    dbg_d = nc.dram_tensor("dbg", [128, 10112], F32, kind="ExternalOutput").ap()
    x1s_d = nc.dram_tensor("x1scr", [T, DM], F32, kind="ExternalOutput").ap()
    wqds_d = nc.dram_tensor("wqds", [II, DM], BF16, kind="ExternalOutput").ap()

    P = Plan(lockstep=lockstep)
    for s in ("ld", "gld", "vv", "ss", "tt", "so", "gc"):
        P.new_sem(s)

    ctx = ExitStack()
    SB = lambda n, sh, dt=F32: ctx.enter_context(nc.sbuf_tensor("s_" + n, sh, dt))

    wqTg = SB("wqTg", [128, DC * II], BF16)
    wqTu = SB("wqTu", [128, DC * II], BF16)
    wxTh = SB("wxTh", [128, DC * 96], BF16)
    wxTl = SB("wxTl", [128, DC * 96], BF16)
    wdtT = SB("wdtT", [RK, DM], BF16)
    bdt = SB("bdt", [128, DC])
    dcol = SB("dcol", [128, DC])
    alog = SB("alog", [128, DC * NS])
    aneg = SB("aneg", [128, DC * NS])
    eyeb = SB("eyeb", [128, 128], BF16)
    eyef = SB("eyef", [128, 128], F32)
    n_mag = SB("n_mag", [128, 1], I32)
    epsc = SB("epsc", [128, 1])
    state = SB("state", [128, DC * NS])
    bnbuf = SB("bnbuf", [128, 12])
    bnagg = SB("bnagg", [128, 2])
    rstd = SB("rstd", [128, 1])
    lntmp = SB("lntmp", [128, 1])
    n_b2 = SB("n_b2", [128, 1], I32)
    n_y = SB("n_y", [128, 1])
    n_t1 = SB("n_t1", [128, 1])
    n_s = SB("n_s", [128, 1])

    AR1 = SB("AR1", [128, 15104])
    AR2 = SB("AR2", [128, 5632])
    AR3 = SB("AR3", [128, 7680])

    def cv(ar, c0, c1, dt=F32, p0=0, p1=128):
        v = ar[p0:p1, c0:c1]
        return v if dt == F32 else v.bitcast(dt)

    QW = dict(kf=cv(AR1, 0, 2816, I32), ku=cv(AR1, 2816, 5632, I32),
              kv=cv(AR1, 5632, 8448, I32), acc=cv(AR1, 8448, 11264),
              tmp=cv(AR1, 11264, 14080), m=cv(AR1, 14080, 14256),
              sc=cv(AR1, 14256, 14432), rs=cv(AR1, 14432, 14608),
              hs=cv(AR1, 14608, 14784), cap=2816)
    QF = dict(kf=cv(AR1, 0, 1024, I32), ku=cv(AR1, 1024, 2048, I32),
              kv=cv(AR1, 2048, 3072, I32), acc=cv(AR1, 3072, 4096),
              tmp=cv(AR1, 4096, 5120), m=cv(AR1, 5120, 5184),
              sc=cv(AR1, 5184, 5248), rs=cv(AR1, 5248, 5312),
              hs=cv(AR1, 5312, 5376), cap=1024)
    wq_sl = [cv(AR1, 5376 + 512 * i, 5888 + 512 * i, BF16) for i in range(19)] + \
            [cv(AR2, 3072 + 512 * i, 3584 + 512 * i, BF16) for i in range(3)]

    xfm32 = cv(AR1, 0, 2048)
    dtfm = cv(AR1, 2048, 4096)
    ufm = cv(AR1, 4096, 6144)
    yfm = cv(AR1, 6144, 8192)
    xfmhi = cv(AR1, 8192, 9216, BF16)
    xfmlo = cv(AR1, 12976, 14000, BF16)
    abar = [cv(AR1, 9216 + 260 * i, 9473 + 260 * i) for i in range(2)]
    tokb = [cv(AR1, 9736 + 260 * i, 9993 + 260 * i) for i in range(2)]
    hbuf = [cv(AR1, 10256 + 260 * i, 10513 + 260 * i) for i in range(2)]
    zbuf = [cv(AR1, 10776 + 260 * i, 11032 + 260 * i) for i in range(2)]
    bmvf = cv(AR1, 11296, 11552, F32, 0, 32)
    cmvf = cv(AR1, 11552, 11808, F32, 0, 32)
    bm_t = [cv(AR1, 11808 + 128 * j, 11936 + 128 * j, BF16, 0, NS) for j in range(3)]
    cm_t = [cv(AR1, 12192, 12320, BF16, 0, NS),
            cv(AR1, 12720, 12848, BF16, 0, NS),
            cv(AR1, 12848, 12976, BF16, 0, NS)]
    xdl = cv(AR1, 12336, 12464, BF16, 0, RK)
    e1v = cv(AR1, 12464, 12720)
    spl = cv(AR1, 12464, 12720, F32, 0, NS)

    wraw = [cv(AR2, 2816 * i, 2816 * (i + 1)) for i in range(2)]
    xtm = [cv(AR2, 1024 * i, 1024 * (i + 1)) for i in range(2)]
    x1t = [cv(AR2, 2048 + 1024 * i, 3072 + 1024 * i) for i in range(2)]
    f_x1 = [cv(AR2, 1024 * i, 1024 * (i + 1)) for i in range(2)]
    f_o = cv(AR2, 2048, 3072)

    q_q = cv(AR3, 0, 1408, BF16)
    selm = cv(AR3, 6240, 7264, BF16, 0, NS)
    abig = [cv(AR3, 0, 2080), cv(AR3, 2080, 4160)]
    tbig = cv(AR3, 4160, 6240)
    hbig = cv(AR1, 9216, 11296)
    f_h1 = cv(AR3, 0, 2816)
    f_r2 = cv(AR3, 2816, 3840)
    f_hq = cv(AR3, 3840, 5248, BF16)
    f_hqT = cv(AR3, 5248, 6656, BF16)
    f_xq = cv(AR3, 6656, 7168, BF16)
    f_xqT = cv(AR3, 7168, 7680, BF16)
    wdev = cv(AR3, 1408, 1920, BF16)

    ps = ctx.enter_context(nc.psum_tensor("ps", [128, 4096], F32))
    psb = lambda b: ps[:, 512 * b: 512 * (b + 1)]
    psbf = lambda b: ps[:, 512 * b: 512 * (b + 1)].bitcast(BF16)

    fls = SB("fls", [128, 4])

    def vfence():
        P.wait("vector", "vv")
        P.op("vector", lambda E: E.tensor_copy(out=fls[:, 0:1], in_=epsc[:]), "vv", 1)

    def sfence():
        P.wait("scalar", "ss")
        P.op("scalar", lambda E: E.copy(out=fls[:, 1:2], in_=epsc[:]), "ss", 1)

    def gfence():
        P.wait("gpsimd", "gc")
        P.op("gpsimd", lambda E: E.tensor_copy(out=fls[:, 2:3], in_=epsc[:]), "gc", 1)

    P.op("sync", lambda E: E.dma_start(out=bdt[:], in_=bdt_d[:]), "ld", 16)
    P.op("sync", lambda E: E.dma_start(out=dcol[:], in_=dvec_d[:]), "ld", 16)
    P.op("sync", lambda E: E.dma_start(out=alog[:], in_=alog_d[:]), "ld", 16)
    P.op("sync", lambda E: E.dma_start(out=eyeb[:], in_=eyeb_d[:]), "ld", 16)
    P.op("sync", lambda E: E.dma_start(out=eyef[:], in_=eyef_d[:]), "ld", 16)
    P.op("sync", lambda E: E.dma_start(out=selm, in_=selb_d[:]), "ld", 16)
    P.op("sync", lambda E: E.dma_start(out=n_mag[:], in_=magic_d[:]), "ld", 16)
    P.op("sync", lambda E: E.dma_start(out=epsc[:], in_=eps_d[:]), "ld", 16)
    wxTtmp = cv(AR3, 0, DC * 96)
    P.op("sync", lambda E: E.dma_start(out=wxTtmp, in_=wxT_d[:]), "ld", 16)
    P.op("gpsimd", lambda E: E.dma_start(out=wdtT[:], in_=wdtT_d[:]), "gld", 16)
    P.wait("scalar", "ld")
    P.op("scalar", lambda E: E.activation(out=aneg[:], in_=alog[:], func=AF.Exp), "ss", 1)
    P.wait("vector", "ss")
    P.op("vector", lambda E: E.tensor_scalar(out=aneg[:], in0=aneg[:], scalar1=-1.0,
                                             scalar2=None, op0=ALU.mult), "vv", 1)
    P.op("vector", lambda E: E.memset(state[:], 0.0), "vv", 1)
    P.wait("vector", "ld")
    P.op("vector", lambda E: E.tensor_copy(out=wxTh[:], in_=wxTtmp), "vv", 1)
    P.op("vector", lambda E: E.tensor_tensor(out=wxTl[:], in0=wxTtmp, in1=wxTh[:],
                                             op=ALU.subtract), "vv", 1)
    vfence()
    P.wait("scalar", "vv")

    def quant(src_ap, dst_ap, nfree, qs=None):
        qs = qs or QW
        for c0 in range(0, nfree, qs["cap"]):
            cw = min(qs["cap"], nfree - c0)
            nblk = cw // 16
            mm_, sc_ = qs["m"][:, 0:nblk], qs["sc"][:, 0:nblk]
            rs_, hs_ = qs["rs"][:, 0:nblk], qs["hs"][:, 0:nblk]
            xn_ = qs["acc"][:, 0:cw]
            kf_, ku_, kv_ = qs["kf"][:, 0:cw], qs["ku"][:, 0:cw], qs["kv"][:, 0:cw]
            acc_, tmp_ = qs["acc"][:, 0:cw], qs["tmp"][:, 0:cw]
            src3 = src_ap[:, c0:c0 + cw].rearrange("p (b k) -> p b k", k=16)
            dst3 = dst_ap[:, c0:c0 + cw].rearrange("p (b k) -> p b k", k=16)
            rs_b = rs_.unsqueeze(2).broadcast_to((128, nblk, 16))
            hs_b = hs_.unsqueeze(2).broadcast_to((128, nblk, 16))

            P.op("vector", lambda E, mm_=mm_, src3=src3: E.tensor_reduce(
                out=mm_, in_=src3, axis=AX.X, op=ALU.max,
                apply_absolute_value=True), "vv", 1)
            P.wait("vector", "vv")
            P.op("vector", lambda E, sc_=sc_, mm_=mm_: E.tensor_scalar(
                out=sc_, in0=mm_, scalar1=1.0 / 6.0,
                scalar2=1e-12, op0=ALU.mult, op1=ALU.max), "vv", 1)
            P.wait("vector", "vv")
            P.op("vector", lambda E, rs_=rs_, sc_=sc_: E.reciprocal(out=rs_, in_=sc_), "vv", 1)
            P.op("vector", lambda E, hs_=hs_, sc_=sc_: E.tensor_scalar(
                out=hs_, in0=sc_, scalar1=0.5, scalar2=None, op0=ALU.mult), "vv", 1)
            P.wait("vector", "vv")
            P.op("vector", lambda E, xn_=xn_, src3=src3, rs_b=rs_b: E.tensor_tensor(
                out=xn_.rearrange("p (b k) -> p b k", k=16),
                in0=src3, in1=rs_b, op=ALU.mult), "vv", 1)
            P.wait("scalar", "vv")
            P.op("scalar", lambda E, kf_=kf_, xn_=xn_: E.activation(
                out=kf_, in_=xn_, func=AF.Identity, scale=2.0), "ss", 1)
            P.op("scalar", lambda E, ku_=ku_, xn_=xn_: E.activation(
                out=ku_, in_=xn_, func=AF.Identity, scale=1.0), "ss", 1)
            P.op("scalar", lambda E, kv_=kv_, xn_=xn_: E.activation(
                out=kv_, in_=xn_, func=AF.Identity, scale=0.5), "ss", 1)
            P.wait("vector", "ss")
            P.op("vector", lambda E, acc_=acc_, kf_=kf_: E.tensor_scalar(
                out=acc_, in0=kf_, scalar1=4.0, scalar2=-4.0,
                op0=ALU.min, op1=ALU.max), "vv", 1)
            for sk, lo, mul in ((ku_, 4.0, 2.0), (ku_, 2.0, -2.0),
                                (kv_, 3.0, 4.0), (kv_, 2.0, -4.0)):
                P.op("vector", lambda E, sk=sk, lo=lo, tmp_=tmp_: E.tensor_scalar(
                    out=tmp_, in0=sk, scalar1=lo, scalar2=-lo, op0=ALU.min, op1=ALU.max), "vv", 1)
                P.op("vector", lambda E, mul=mul, acc_=acc_, tmp_=tmp_: E.scalar_tensor_tensor(
                    out=acc_, in0=tmp_, scalar=mul, in1=acc_, op0=ALU.mult, op1=ALU.add), "vv", 1)
            P.op("vector", lambda E, dst3=dst3, acc_=acc_, hs_b=hs_b: E.tensor_tensor(
                out=dst3, in0=acc_.rearrange("p (b k) -> p b k", k=16),
                in1=hs_b, op=ALU.mult), "vv", 1)
        vfence()

    jobs = []
    if "W" in phases:
        for wi, (wsrc, is_down) in enumerate(((wg_d, False), (wu_d, False), (wd_d, True))):
            for ni in range(DC if is_down else NI):
                jobs.append((wi, wsrc, is_down, ni))
    quant_done = []
    for j, (wi, wsrc, is_down, ni) in enumerate(jobs):
        buf = wraw[j % 2]
        nfree = II if is_down else DM
        if j >= 2:
            P.wait("sync", "vv", quant_done[j - 2])
        P.op("sync", lambda E, buf=buf, wsrc=wsrc, ni=ni, nf=nfree:
             E.dma_start(out=buf[:, 0:nf], in_=wsrc[ni * 128:(ni + 1) * 128, :]), "ld", 16)
        P.wait("vector", "ld")
        P.wait("vector", "tt")
        P.wait("vector", "gld")
        quant(buf[:, 0:nfree], q_q[:, 0:nfree], nfree)
        quant_done.append(P.mark("vv"))
        ntp = nfree // 128
        P.wait("tensor", "vv", quant_done[-1])
        for g0 in range(0, ntp, 8):
            gn = min(8, ntp - g0)
            if g0 > 0:
                P.wait("tensor", "vv")
            for k in range(gn):
                kk = g0 + k
                P.op("tensor", lambda E, kk=kk, k=k, f=(k == 0), l=(k == gn - 1):
                     E.matmul(out=psbf(6)[:, 128 * k:128 * (k + 1)],
                              lhsT=q_q[:, 128 * kk:128 * (kk + 1)], rhs=eyeb[:],
                              is_transpose=True, start=f, stop=l),
                     "tt", 1 if k == gn - 1 else 0)
            P.wait("vector", "tt")
            if not is_down:
                dst = wqTg if wi == 0 else wqTu
                def ev(E, dst=dst, g0=g0, gn=gn, ni=ni):
                    o = dst[:].rearrange("p (c n) -> p c n", n=II)[:, g0:g0 + gn,
                                                                   ni * 128:(ni + 1) * 128]
                    return E.tensor_copy(out=o, in_=psbf(6)[:, 0:128 * gn]
                                         .rearrange("p (c n) -> p c n", n=128))
                P.op("vector", ev, "vv", 1)
            else:
                P.wait("vector", "gld")
                P.op("vector", lambda E, gn=gn: E.tensor_copy(
                    out=wdev[:, 0:128 * gn], in_=psbf(6)[:, 0:128 * gn]), "vv", 1)
                vfence()
                P.wait("gpsimd", "vv")
                def st(E, g0=g0, gn=gn, ni=ni):
                    o = wqds_d.rearrange("(g j) d -> j g d", j=128)[:, g0:g0 + gn,
                                                                   ni * 128:(ni + 1) * 128]
                    return E.dma_start(out=o, in_=wdev[:, 0:128 * gn]
                                       .rearrange("p (g n) -> p g n", n=128))
                P.op("gpsimd", st, "gld", 16)

    store_mark = [0]
    for tc in range(min(NTC, m_tc) if "M" in phases else 0):
        t0 = tc * TCS
        P.wait("sync", "vv")
        P.wait("sync", "gld")
        P.wait("vector", "tt")
        P.wait("vector", "so", store_mark[-1])
        P.wait("tensor", "vv")
        P.wait("tensor", "ss")
        for i in range(2):
            P.op("sync", lambda E, i=i, t0=t0:
                 E.dma_start(out=xtm[i], in_=x_d[t0 + 128 * i: t0 + 128 * (i + 1), :]),
                 "ld", 16)
        P.wait("tensor", "ld")
        if m_stage < -4:
            continue
        for dc in range(DC):
            b = dc % 2
            if dc >= 2:
                P.wait("tensor", "vv")
                P.wait("tensor", "ss")
            if m_stage < -3 and dc >= 1:
                continue
            for i in range(2):
                P.op("tensor", lambda E, dc=dc, i=i, b=b:
                     E.matmul(out=psb(b)[:, 128 * i:128 * (i + 1)],
                              lhsT=xtm[i][:, 128 * dc:128 * (dc + 1)], rhs=eyef[:],
                              is_transpose=True, start=(i == 0), stop=(i == 1)),
                     "tt", 1 if i == 1 else 0)
            P.wait("scalar", "tt")
            P.op("scalar", lambda E, dc=dc, b=b:
                 E.copy(out=xfm32[:, TCS * dc: TCS * (dc + 1)], in_=psb(b)[:, 0:TCS]), "ss", 1)
        sfence()
        if m_stage < -2:
            continue
        P.wait("vector", "ss")
        P.op("vector", lambda E: E.tensor_copy(out=xfmhi[:], in_=xfm32[:]), "vv", 1)
        P.op("vector", lambda E: E.tensor_tensor(out=xfmlo[:], in0=xfm32[:], in1=xfmhi[:],
                                                 op=ALU.subtract), "vv", 1)
        vfence()
        P.wait("tensor", "ss")
        P.wait("tensor", "vv")
        for j, (wop, xop) in enumerate(((wxTh, xfmhi), (wxTh, xfmlo), (wxTl, xfmhi))):
            for dc in range(DC):
                P.op("tensor", lambda E, dc=dc, wop=wop, xop=xop, j=j:
                     E.matmul(out=psb(2)[0:80, 0:TCS], lhsT=wop[:, 96 * dc:96 * dc + 80],
                              rhs=xop[:, TCS * dc:TCS * (dc + 1)],
                              start=(dc == 0 and j == 0), stop=(dc == DC - 1 and j == 2)),
                     "tt", 0)
                P.op("tensor", lambda E, dc=dc, wop=wop, xop=xop, j=j:
                     E.matmul(out=psb(3)[0:16, 0:TCS], lhsT=wop[:, 96 * dc + 80:96 * (dc + 1)],
                              rhs=xop[:, TCS * dc:TCS * (dc + 1)],
                              start=(dc == 0 and j == 0), stop=(dc == DC - 1 and j == 2)),
                     "tt", 1 if (dc == DC - 1 and j == 2) else 0)
        P.wait("vector", "tt")
        P.op("vector", lambda E: E.tensor_copy(out=xdl[:, 0:TCS], in_=psb(2)[0:RK, 0:TCS]),
             "vv", 1)
        P.op("vector", lambda E: E.tensor_copy(out=bmvf[:, 0:TCS], in_=psb(2)[RK:96, 0:TCS]),
             "vv", 1)
        P.op("vector", lambda E: E.tensor_copy(out=cmvf[:, 0:TCS], in_=psb(3)[0:32, 0:TCS]),
             "vv", 1)
        P.wait("vector", "vv")
        for vf, terms in ((bmvf, bm_t), (cmvf, cm_t)):
            P.op("vector", lambda E, vf=vf, t0_=terms[0]:
                 E.tensor_copy(out=t0_[:, 0:TCS], in_=vf[0:NS, 0:TCS]), "vv", 1)
            P.wait("vector", "vv")
            P.op("vector", lambda E, vf=vf, t0_=terms[0]:
                 E.tensor_tensor(out=spl[:, 0:TCS], in0=vf[0:NS, 0:TCS], in1=t0_[:, 0:TCS],
                                 op=ALU.subtract), "vv", 1)
            P.wait("vector", "vv")
            P.op("vector", lambda E, t1_=terms[1]:
                 E.tensor_copy(out=t1_[:, 0:TCS], in_=spl[:, 0:TCS]), "vv", 1)
            P.wait("vector", "vv")
            P.op("vector", lambda E, t1_=terms[1], t2_=terms[2]:
                 E.tensor_tensor(out=t2_[:, 0:TCS], in0=spl[:, 0:TCS], in1=t1_[:, 0:TCS],
                                 op=ALU.subtract), "vv", 1)
        vfence()
        if m_stage < -1:
            continue
        P.wait("tensor", "ss")
        P.wait("tensor", "vv")
        for dc in range(DC):
            b = 3 if dc % 2 == 0 else 2
            if dc >= 2:
                P.wait("tensor", "ss")
            P.op("tensor", lambda E, dc=dc, b=b:
                 E.matmul(out=psb(b)[:, 0:TCS], lhsT=wdtT[:, 128 * dc:128 * (dc + 1)],
                          rhs=xdl[:, 0:TCS], start=True, stop=True), "tt", 1)
            P.wait("scalar", "tt")
            P.op("scalar", lambda E, dc=dc, b=b:
                 E.activation(out=e1v[:], in_=psb(b)[:, 0:TCS], func=AF.Exp,
                              bias=bdt[:, dc:dc + 1], scale=1.0), "ss", 1)
            P.op("scalar", lambda E, dc=dc:
                 E.activation(out=dtfm[:, TCS * dc:TCS * (dc + 1)], in_=e1v[:], func=AF.Ln,
                              bias=1.0, scale=1.0), "ss", 1)
        sfence()
        P.wait("vector", "ss")
        P.op("vector", lambda E: E.tensor_tensor(out=ufm[:], in0=dtfm[:], in1=xfm32[:],
                                                 op=ALU.mult), "vv", 1)
        if m_stage < 1:
            continue
        if tc == 0:
            for i in range(2):
                P.op("vector", lambda E, i=i: E.memset(
                    abig[i].rearrange("p (c n) -> p c n", n=260)[:, :, 0:1], 0.0), "vv", 1)
        tb3 = tbig.rearrange("p (c n) -> p c n", n=260)
        hb3 = hbig.rearrange("p (c n) -> p c n", n=260)
        uf3 = ufm.rearrange("p (c n) -> p c n", n=TCS)
        yf3 = yfm.rearrange("p (c n) -> p c n", n=TCS)
        st3 = state.rearrange("p (c n) -> p c n", n=NS)
        scan_marks = {}
        zread_marks = {}
        bc_marks = {}
        for s in range(NS):
            pb = s % 2
            ab = abig[pb]
            ab3 = ab.rearrange("p (c n) -> p c n", n=260)
            if s >= 2:
                P.wait("tensor", "vv", zread_marks[s - 2])
            for bank, terms in ((4 + pb, bm_t), (6 + pb, cm_t)):
                for j in range(3):
                    P.op("tensor", lambda E, s=s, bank=bank, tj=terms[j],
                         f=(j == 0), l=(j == 2):
                         E.matmul(out=psb(bank)[:, 0:TCS], lhsT=selm[:, 128 * s:128 * (s + 1)],
                                  rhs=tj[:, 0:TCS], start=f, stop=l),
                         "tt", 1 if j == 2 else 0)
            bc_marks[s] = P.mark("tt")
            if s >= 2:
                P.wait("scalar", "vv", scan_marks[s - 2])
            for dc in range(DC):
                P.op("scalar", lambda E, dc=dc, s=s, ab3=ab3:
                     E.activation(out=ab3[:, dc, 1:TCS + 1],
                                  in_=dtfm[:, TCS * dc:TCS * (dc + 1)],
                                  func=AF.Exp, scale=aneg[:, dc * NS + s:dc * NS + s + 1]),
                     "ss", 1)
            sfence()
            P.op("vector", lambda E, s=s:
                 E.tensor_copy(out=tb3[:, :, 0:1], in_=st3[:, :, s:s + 1]), "vv", 1)
            P.wait("vector", "tt", bc_marks[s])
            P.op("vector", lambda E, pb=pb:
                 E.tensor_tensor(out=tb3[:, :, 1:TCS + 1], in0=uf3[:, :, :],
                                 in1=psb(4 + pb)[:, 0:TCS].unsqueeze(1)
                                 .broadcast_to((128, DC, TCS)),
                                 op=ALU.mult), "vv", 1)
            P.wait("vector", "ss")
            for dc in range(DC):
                P.op("vector", lambda E, dc=dc, ab=ab:
                     E.tensor_tensor_scan(out=hbig[:, 260 * dc:260 * dc + 257],
                                          data0=ab[:, 260 * dc:260 * dc + 257],
                                          data1=tbig[:, 260 * dc:260 * dc + 257],
                                          initial=0.0, op0=ALU.mult, op1=ALU.add), "vv", 1)
            scan_marks[s] = P.mark("vv")
            P.op("vector", lambda E, pb=pb:
                 E.tensor_tensor(out=tb3[:, :, 1:TCS + 1], in0=hb3[:, :, 1:TCS + 1],
                                 in1=psb(6 + pb)[:, 0:TCS].unsqueeze(1)
                                 .broadcast_to((128, DC, TCS)),
                                 op=ALU.mult), "vv", 1)
            zread_marks[s] = P.mark("vv")
            P.op("vector", lambda E, s=s:
                 E.tensor_copy(out=st3[:, :, s:s + 1], in_=hb3[:, :, TCS:TCS + 1]), "vv", 1)
            if s == 0:
                P.op("vector", lambda E:
                     E.tensor_copy(out=yf3[:, :, :], in_=tb3[:, :, 1:TCS + 1]), "vv", 1)
            else:
                P.op("vector", lambda E:
                     E.tensor_tensor(out=yf3[:, :, :], in0=yf3[:, :, :],
                                     in1=tb3[:, :, 1:TCS + 1], op=ALU.add), "vv", 1)
        if m_stage < 2:
            continue
        for dc in range(DC):
            P.op("vector", lambda E, dc=dc:
                 E.scalar_tensor_tensor(out=yfm[:, TCS * dc:TCS * (dc + 1)],
                                        in0=xfm32[:, TCS * dc:TCS * (dc + 1)],
                                        scalar=dcol[:, dc:dc + 1],
                                        in1=yfm[:, TCS * dc:TCS * (dc + 1)],
                                        op0=ALU.mult, op1=ALU.add), "vv", 1)
        vfence()
        P.wait("tensor", "vv")
        for dc in range(DC):
            b = dc % 2
            if dc >= 2:
                P.wait("tensor", "vv")
            for i in range(2):
                P.op("tensor", lambda E, dc=dc, i=i, b=b:
                     E.matmul(out=psb(b)[:, 128 * i:128 * (i + 1)],
                              lhsT=yfm[:, TCS * dc + 128 * i: TCS * dc + 128 * (i + 1)],
                              rhs=eyef[:], is_transpose=True, start=(i == 0), stop=(i == 1)),
                     "tt", 1 if i == 1 else 0)
            P.wait("vector", "tt")
            for i in range(2):
                P.op("vector", lambda E, dc=dc, i=i, b=b:
                     E.tensor_tensor(out=xtm[i][:, 128 * dc:128 * (dc + 1)],
                                     in0=psb(b)[:, 128 * i:128 * (i + 1)],
                                     in1=xtm[i][:, 128 * dc:128 * (dc + 1)], op=ALU.add),
                     "vv", 1)
        if tc == dbg_tc:
            P.wait("gpsimd", "vv")
            P.op("gpsimd", lambda E: E.dma_start(out=dbg_d[:, 8576:9600], in_=xtm[0]), "so", 16)
            P.op("gpsimd", lambda E: E.dma_start(out=dbg_d[:, 9600:10112],
                                                 in_=xtm[1][:, 0:512]), "so", 16)
            P.wait("vector", "so")
        for i in range(2):
            P.op("vector", lambda E, i=i: E.bn_stats(out=bnbuf[:, 0:6], in_=xtm[i][:, 0:512]),
                 "vv", 1)
            P.op("vector", lambda E, i=i: E.bn_stats(out=bnbuf[:, 6:12], in_=xtm[i][:, 512:1024]),
                 "vv", 1)
            P.wait("vector", "vv")
            P.op("vector", lambda E: E.bn_aggr(out=bnagg[:], in_=bnbuf[:]), "vv", 1)
            vfence()
            P.wait("scalar", "vv")
            P.op("scalar", lambda E: E.activation(out=lntmp[:], in_=bnagg[:, 1:2], func=AF.Ln,
                                                  bias=epsc[:], scale=1.0), "ss", 1)
            P.wait("scalar", "ss")
            P.op("scalar", lambda E: E.activation(out=rstd[:], in_=lntmp[:], func=AF.Exp,
                                                  scale=-0.5), "ss", 1)
            sfence()
            if tc == dbg_tc and i == 0:
                P.wait("gpsimd", "ss")
                P.wait("gpsimd", "vv")
                P.op("gpsimd", lambda E: E.dma_start(out=dbg_d[:, 8320:8322], in_=bnagg[:]),
                     "so", 16)
                P.op("gpsimd", lambda E: E.dma_start(out=dbg_d[:, 8322:8323], in_=rstd[:], allow_slow_non_contiguous=True),
                     "so", 16)
                P.wait("vector", "so")
                P.wait("scalar", "so")
            P.wait("vector", "ss")
            P.wait("vector", "vv")
            P.op("vector", lambda E, i=i:
                 E.tensor_scalar(out=x1t[i], in0=xtm[i], scalar1=bnagg[:, 0:1],
                                 scalar2=rstd[:], op0=ALU.subtract, op1=ALU.mult), "vv", 1)
            vfence()
            P.wait("gpsimd", "vv")
            P.op("gpsimd", lambda E, i=i, t0=t0:
                 E.dma_start(out=x1s_d[t0 + 128 * i: t0 + 128 * (i + 1), :], in_=x1t[i]),
                 "so", 16)
        store_mark.append(P.mark("so"))

    nbw = [512, 512, 512, 512, 512, 256]
    f_resid_mark = []
    f_store_mark = []

    def f_head(t2):
        fb = f_x1[t2 % 2]
        if t2 >= 2:
            P.wait("sync", "vv", f_resid_mark[t2 - 2])
        if t2 == 0:
            P.wait("sync", "so")
        P.op("sync", lambda E, t2=t2, fb=fb:
             E.dma_start(out=fb, in_=x1s_d[128 * t2:128 * (t2 + 1), :]), "ld", 16)
        P.wait("vector", "ld")
        if t2 == 0:
            P.wait("vector", "tt")
            P.wait("vector", "so", store_mark[-1])
        quant(fb, f_xq, DM, QF)
        P.wait("tensor", "vv")
        for k in range(DC):
            P.op("tensor", lambda E, k=k:
                 E.matmul(out=psbf(5)[:, 128 * k:128 * (k + 1)],
                          lhsT=f_xq[:, 128 * k:128 * (k + 1)], rhs=eyeb[:],
                          is_transpose=True, start=(k == 0), stop=(k == DC - 1)),
                 "tt", 1 if k == DC - 1 else 0)
        P.wait("vector", "tt")
        P.op("vector", lambda E: E.tensor_copy(out=f_xqT[:], in_=psbf(5)[:, 0:DM]), "vv", 1)
        vfence()
        f_head_mark[t2] = P.mark("vv")

    f_head_mark = {}
    wdld_mark = [0]
    if "F" in phases:
        f_head(0)
        P.wait("sync", "gld")
        for ic in range(NI):
            P.op("sync", lambda E, ic=ic: E.dma_start(
                out=wq_sl[ic], in_=wqds_d[128 * ic:128 * (ic + 1), :]), "ld", 16)
        wdld_mark[0] = P.mark("ld")
    for t2 in range(NT128 if "F" in phases else 0):
        fb = f_x1[t2 % 2]
        P.wait("tensor", "vv", f_head_mark[t2])
        for dc in range(DC):
            for nb in range(6):
                P.op("tensor", lambda E, dc=dc, nb=nb, w=nbw[nb]:
                     E.matmul(out=psb(nb)[:, 0:w], lhsT=f_xqT[:, 128 * dc:128 * (dc + 1)],
                              rhs=wqTg[:, II * dc + 512 * nb: II * dc + 512 * nb + w],
                              start=(dc == 0), stop=(dc == DC - 1)),
                     "tt", 1 if (dc == DC - 1 and nb == 5) else 0)
        P.wait("scalar", "tt")
        for c0 in range(0, II, 1024):
            cw = min(1024, II - c0)
            P.op("scalar", lambda E, c0=c0, cw=cw: E.activation(
                out=QF["acc"][:, 0:cw], in_=ps[:, c0:c0 + cw],
                func=AF.Exp, scale=-1.0), "ss", 1)
            P.op("scalar", lambda E, c0=c0, cw=cw: E.activation(
                out=QF["tmp"][:, 0:cw], in_=QF["acc"][:, 0:cw],
                func=AF.Ln, bias=1.0, scale=1.0), "ss", 1)
            P.op("scalar", lambda E, c0=c0, cw=cw: E.activation(
                out=f_h1[:, c0:c0 + cw], in_=QF["tmp"][:, 0:cw],
                func=AF.Exp, scale=-1.0), "ss", 1)
        P.wait("vector", "ss")
        P.op("vector", lambda E: E.scalar_tensor_tensor(out=f_h1[:], in0=f_h1[:], scalar=1.0,
                                                        in1=ps[:, 0:II], op0=ALU.mult,
                                                        op1=ALU.mult), "vv", 1)
        P.wait("tensor", "vv")
        for dc in range(DC):
            for nb in range(6):
                P.op("tensor", lambda E, dc=dc, nb=nb, w=nbw[nb]:
                     E.matmul(out=psb(nb)[:, 0:w], lhsT=f_xqT[:, 128 * dc:128 * (dc + 1)],
                              rhs=wqTu[:, II * dc + 512 * nb: II * dc + 512 * nb + w],
                              start=(dc == 0), stop=(dc == DC - 1)),
                     "tt", 1 if (dc == DC - 1 and nb == 5) else 0)
        P.wait("vector", "tt")
        P.op("vector", lambda E: E.tensor_tensor(out=f_h1[:], in0=f_h1[:], in1=ps[:, 0:II],
                                                 op=ALU.mult), "vv", 1)
        quant(f_h1[:], f_hq, II, QF)
        P.wait("tensor", "vv")
        for g0 in range(0, NI, 8):
            gn = min(8, NI - g0)
            bb = g0 // 8
            for k in range(gn):
                kk = g0 + k
                P.op("tensor", lambda E, kk=kk, k=k, bb=bb, f=(k == 0), l=(k == gn - 1):
                     E.matmul(out=psbf(bb)[:, 128 * k:128 * (k + 1)],
                              lhsT=f_hq[:, 128 * kk:128 * (kk + 1)], rhs=eyeb[:],
                              is_transpose=True, start=f, stop=l),
                     "tt", 1 if k == gn - 1 else 0)
        P.wait("vector", "tt")
        for g0 in range(0, NI, 8):
            gn = min(8, NI - g0)
            bb = g0 // 8
            P.op("vector", lambda E, g0=g0, gn=gn, bb=bb:
                 E.tensor_copy(out=f_hqT[:, 128 * g0:128 * (g0 + gn)],
                               in_=psbf(bb)[:, 0:128 * gn]), "vv", 1)
        vfence()
        P.wait("tensor", "vv")
        if t2 == 0:
            P.wait("tensor", "ld", wdld_mark[0])
        for ic in range(NI):
            for half in range(2):
                P.op("tensor", lambda E, ic=ic, half=half:
                     E.matmul(out=psb(6 + half)[:, 0:512],
                              lhsT=f_hqT[:, 128 * ic:128 * (ic + 1)],
                              rhs=wq_sl[ic][:, 512 * half:512 * (half + 1)],
                              start=(ic == 0), stop=(ic == NI - 1)),
                     "tt", 1 if half == 1 else 0)
        dn_mark = P.mark("tt")
        if t2 + 1 < NT128:
            f_head(t2 + 1)
        P.wait("vector", "tt", dn_mark)
        P.op("vector", lambda E, fb=fb: E.tensor_tensor(out=f_r2[:], in0=ps[:, 3072:4096],
                                                        in1=fb, op=ALU.add), "vv", 1)
        f_resid_mark.append(P.mark("vv"))
        P.op("vector", lambda E: E.bn_stats(out=bnbuf[:, 0:6], in_=f_r2[:, 0:512]), "vv", 1)
        P.op("vector", lambda E: E.bn_stats(out=bnbuf[:, 6:12], in_=f_r2[:, 512:1024]), "vv", 1)
        P.wait("vector", "vv")
        P.op("vector", lambda E: E.bn_aggr(out=bnagg[:], in_=bnbuf[:]), "vv", 1)
        vfence()
        P.wait("scalar", "vv")
        P.op("scalar", lambda E: E.activation(out=lntmp[:], in_=bnagg[:, 1:2], func=AF.Ln,
                                              bias=epsc[:], scale=1.0), "ss", 1)
        P.wait("scalar", "ss")
        P.op("scalar", lambda E: E.activation(out=rstd[:], in_=lntmp[:], func=AF.Exp,
                                              scale=-0.5), "ss", 1)
        sfence()
        P.wait("vector", "ss")
        if t2 >= 1:
            P.wait("vector", "so", f_store_mark[t2 - 1])
        P.op("vector", lambda E: E.tensor_scalar(out=f_o, in0=f_r2[:],
                                                 scalar1=bnagg[:, 0:1], scalar2=rstd[:],
                                                 op0=ALU.subtract, op1=ALU.mult), "vv", 1)
        vfence()
        P.wait("gpsimd", "vv")
        P.op("gpsimd", lambda E, t2=t2:
             E.dma_start(out=out_d[128 * t2:128 * (t2 + 1), :], in_=f_o), "so", 16)
        f_store_mark.append(P.mark("so"))

    P.wait("gpsimd", "so")

    with ExitStack() as ectx:
        sems = {s: ectx.enter_context(nc.semaphore(s)) for s in P.cnt}
        block = ectx.enter_context(nc.Block())

        def runner(name):
            def go(E):
                for it in P.items[name]:
                    if it[0] == "op":
                        _, fn, sem, k = it
                        inst = fn(E)
                        if sem is not None and k > 0:
                            inst.then_inc(sems[sem], k)
                    else:
                        _, sem, v = it
                        E.wait_ge(sems[sem], v)
            return go

        block.sync(runner("sync"))
        block.vector(runner("vector"))
        block.scalar(runner("scalar"))
        block.tensor(runner("tensor"))
        block.gpsimd(runner("gpsimd"))
    ctx.close()
    return nc


def kernel(L0_attn_x, L0_attn_A_log, L0_attn_D, L0_attn_W_xproj, L0_attn_W_dt,
           L0_attn_b_dt, L0_ffn_w_gate, L0_ffn_w_up, L0_ffn_w_down):
    import ml_dtypes
    x = np.ascontiguousarray(np.asarray(L0_attn_x, dtype=np.float32))
    A_log = np.asarray(L0_attn_A_log, np.float32)
    Dv = np.asarray(L0_attn_D, np.float32)
    Wx = np.asarray(L0_attn_W_xproj, np.float32)
    Wdt = np.asarray(L0_attn_W_dt, np.float32)
    bdt = np.asarray(L0_attn_b_dt, np.float32)
    wg = np.ascontiguousarray(np.asarray(L0_ffn_w_gate, np.float32))
    wu = np.ascontiguousarray(np.asarray(L0_ffn_w_up, np.float32))
    wd = np.ascontiguousarray(np.asarray(L0_ffn_w_down, np.float32))

    wxT = np.ascontiguousarray(
        Wx.T.reshape(DC, 128, 96).transpose(1, 0, 2).reshape(128, DC * 96))
    wdtT = np.ascontiguousarray(Wdt.T)
    bdt_l = np.ascontiguousarray(bdt.reshape(DC, 128).T)
    alog_l = np.ascontiguousarray(
        A_log.reshape(DC, 128, NS).transpose(1, 0, 2).reshape(128, DC * NS))
    dvec_l = np.ascontiguousarray(Dv.reshape(DC, 128).T)
    eyeb = np.eye(128, dtype=ml_dtypes.bfloat16)
    eyef = np.eye(128, dtype=np.float32)
    selb = np.zeros((NS, NS * 128), dtype=ml_dtypes.bfloat16)
    for s in range(NS):
        selb[s, s * 128:(s + 1) * 128] = 1.0
    magic = np.full((128, 1), 0x5F3759DF, dtype=np.int32)

    nc = bass.Bass("TRN2", target_bir_lowering=False, detect_race_conditions=False)
    build(nc)

    in_maps = []
    for b in range(8):
        in_maps.append({
            "x": np.ascontiguousarray(x[b]),
            "wxT": wxT, "wdtT": wdtT, "bdt": bdt_l, "alog": alog_l, "dvec": dvec_l,
            "wg": wg, "wu": wu, "wd": wd,
            "eyeb": eyeb, "eyef": eyef, "selb": selb, "magic": magic,
            "epsc": np.full((128, 1), 1e-5, np.float32),
        })
    res = run_bass_kernel_spmd(nc, in_maps, core_ids=list(range(8)))
    return np.stack([r["out"] for r in res.results], axis=0)
